# version 31
# baseline (speedup 1.0000x reference)
import numpy as np
from contextlib import ExitStack

import concourse.bass as bass
import concourse.mybir as mybir
from concourse.bass_utils import run_bass_kernel_spmd

F32 = mybir.dt.float32
BF16 = mybir.dt.bfloat16
FP8 = mybir.dt.float8e4
I32 = mybir.dt.int32
AF = mybir.ActivationFunctionType
ALU = mybir.AluOpType

NCORES = 8
D = 2048
NST = 1024
TOK = 2048
T = 256
NCH = TOK // T
KD = D // 128
KN = NST // 128
ET = 8
EW = 512

ENG_SEM = {"sync": "SD", "vector": "V", "scalar": "A",
           "gpsimd": "P", "tensor": "E"}

C52 = float(3 * 2 ** 52)
C82 = float(3 * 2 ** 81)

QENG_W = {"i1": "vector", "i2": "vector", "inv": "vector", "dd": "vector",
          "tt": "vector", "nr": "vector", "dif": "vector",
          "fin": "vector", "q8": "vector"}
QENG_R = {"i1": "vector", "i2": "vector", "inv": "vector", "dd": "vector",
          "tt": "vector", "nr": "vector", "dif": "vector",
          "fin": "vector", "q8": "vector"}


def build_nc():
    nc = bass.Bass("TRN2", target_bir_lowering=False)
    hb = {}
    hb["x_tm"] = nc.dram_tensor("x_tm", [TOK, D], F32, kind="ExternalInput").ap()
    hb["x_fm"] = nc.dram_tensor("x_fm", [D, TOK], F32, kind="ExternalInput").ap()
    hb["bt"] = nc.dram_tensor("bt", [D, NST], F32, kind="ExternalInput").ap()
    hb["ct"] = nc.dram_tensor("ct", [NST, D], F32, kind="ExternalInput").ap()
    hb["w"] = nc.dram_tensor("w", [D, D], F32, kind="ExternalInput").ap()
    hb["dv"] = nc.dram_tensor("dv", [D, 1], F32, kind="ExternalInput").ap()
    hb["s0"] = nc.dram_tensor("s0", [1, 1], F32, kind="ExternalInput").ap()
    hb["s1"] = nc.dram_tensor("s1", [1, 1], F32, kind="ExternalInput").ap()
    hb["out"] = nc.dram_tensor("out", [D, TOK], F32, kind="ExternalOutput").ap()

    with ExitStack() as st:
        sb = lambda n, s, d: st.enter_context(nc.sbuf_tensor(n, s, d))
        ps = lambda n, s: st.enter_context(nc.psum_tensor(n, s, F32))
        t = {}
        t["b_bf"] = sb("b_bf", [128, KD * NST], BF16)
        t["c_bf"] = sb("c_bf", [128, KN * D], BF16)
        t["wq"] = sb("wq", [128, KD * D], FP8)
        t["xnb0"] = sb("xnb0", [128, KD * T], BF16)
        t["xnb1"] = sb("xnb1", [128, KD * T], BF16)
        t["q10"] = sb("q10", [128, KD * T], FP8)
        t["q11"] = sb("q11", [128, KD * T], FP8)
        t["state"] = sb("state", [128, KN * T], BF16)
        t["st2k"] = [sb(f"st2k{i}", [128, 2 * EW], F32) for i in range(2)]
        t["xf"] = [sb(f"xf{i}", [128, 2 * EW], F32) for i in range(2)]
        t["vq"] = sb("vq", [128, 2 * EW], F32)
        t["ia"] = sb("ia", [128, 2 * EW], I32)
        t["ib"] = sb("ib", [128, 2 * EW], I32)
        t["xn"] = [sb(f"xn{i}", [128, 2 * EW], F32) for i in range(3)]
        t["sbr"] = [sb(f"sbr{i}", [128, 2 * EW], F32) for i in range(2)]
        t["sbn"] = [sb(f"sbn{i}", [128, 2 * EW], F32) for i in range(2)]
        t["me"] = [sb(f"me{i}", [128, EW], F32) for i in range(4)]
        t["ge"] = [sb(f"ge{i}", [128, EW], F32) for i in range(4)]
        t["te"] = [sb(f"te{i}", [128, EW], F32) for i in range(2)]
        t["oe"] = [sb(f"oe{i}", [128, EW], F32) for i in range(2)]
        t["bnb"] = sb("bnb", [128, 48], F32)
        t["mv"] = sb("mv", [128, 4], F32)
        t["vep"] = sb("vep", [128, 2], F32)
        t["rsc"] = sb("rsc", [128, 2], F32)
        t["nmc"] = sb("nmc", [128, 2], F32)
        t["yt"] = sb("yt", [128, 2], F32)
        t["wt"] = sb("wt", [128, 2], F32)
        t["zt"] = sb("zt", [128, 2], F32)
        t["sit"] = sb("sit", [128, 2], I32)
        t["rrow"] = sb("rrow", [1, T], F32)
        t["nrow"] = sb("nrow", [1, T], F32)
        t["ones"] = sb("ones", [1, 128], F32)
        t["iden"] = sb("iden", [128, 128], F32)
        t["s0sb"] = sb("s0sb", [1, 1], F32)
        t["s1sb"] = sb("s1sb", [1, 1], F32)
        t["sssb"] = sb("sssb", [1, 1], F32)
        t["rsssb"] = sb("rsssb", [1, 1], F32)
        t["scol"] = sb("scol", [128, 1], F32)
        t["dcolD"] = sb("dcolD", [128, 1], F32)
        t["ccol"] = sb("ccol", [128, 1], F32)
        t["s1col"] = sb("s1col", [128, 1], F32)
        t["iscol"] = sb("iscol", [128, 1], F32)
        t["dcol"] = sb("dcol", [128, KD], F32)
        t["pm1"] = [ps("pm1a", [128, EW]), ps("pm1b", [128, EW])]
        t["pm2"] = [ps("pm2a", [128, EW]), ps("pm2b", [128, EW])]
        t["pff"] = [ps("pffa", [128, EW]), ps("pffb", [128, EW])]
        t["pbr"] = ps("pbr", [128, EW])
        t["pbn"] = ps("pbn", [128, EW])

        sems = {}
        for name in ("SC", "CN", "K0", "K1", "F0", "F1", "X0", "X1",
                     "P0", "P1", "V", "A", "P", "E"):
            sems[name] = st.enter_context(nc.semaphore(name))

        blk = st.enter_context(nc.Block())
        for eng in ("sync", "vector", "scalar", "gpsimd", "tensor"):
            getattr(blk, eng)(lambda e, eng=eng: emit(nc, t, hb, sems, eng))
    return nc


def emit(nc, t, hb, sems, sel):
    cnt = {k: 0 for k in sems}

    def op(eng, meth, *a, wait=(), inc=None, inck=1, **kw):
        s = inc if inc is not None else ENG_SEM[eng]
        if eng == sel:
            e = getattr(nc, eng)
            for sname, v in wait:
                if v > 0:
                    e.wait_ge(sems[sname], v)
            r = getattr(e, meth)(*a, **kw)
            if s is not False:
                r.then_inc(sems[s], inck)
        if s is False:
            return None
        cnt[s] += inck
        return (s, cnt[s])

    def waitonly(eng, *toks):
        if eng == sel:
            e = getattr(nc, eng)
            for sname, v in toks:
                if v > 0:
                    e.wait_ge(sems[sname], v)

    def dma(eng, sem, dst, src, wait=()):
        return op(eng, "dma_start", wait=wait, inc=sem, inck=16,
                  out=dst, in_=src)

    NOTOK = ("V", 0)

    dma("sync", "SC", t["s0sb"][:], hb["s0"][:])
    tok_s1 = dma("sync", "SC", t["s1sb"][:], hb["s1"][:])
    tok_s0 = tok_s1
    dtoks = []
    for k in range(KD):
        dtoks.append(dma("sync", "CN", t["dcol"][:, k:k + 1],
                         hb["dv"][128 * k:128 * (k + 1), :]))
    tok_ones = op("vector", "memset", t["ones"][:], 1.0)
    tok_d1 = op("vector", "memset", t["dcolD"][:], 1.0)
    tok_d2 = op("vector", "tensor_scalar", out=t["dcolD"][:],
                in0=t["dcolD"][:], scalar1=float(2.0 ** -30), scalar2=1.0,
                op0=ALU.add, op1=ALU.subtract, wait=[tok_d1])
    tok_dD = op("vector", "tensor_scalar", out=t["dcolD"][:],
                in0=t["dcolD"][:], scalar1=float(2.0 ** 30), scalar2=None,
                op0=ALU.mult, wait=[tok_d2])
    tok_c1 = op("vector", "tensor_scalar", out=t["ccol"][:],
                in0=t["dcolD"][:], scalar1=float(C82 - C52), scalar2=None,
                op0=ALU.mult, wait=[tok_dD])
    tok_cc = op("vector", "tensor_scalar", out=t["ccol"][:],
                in0=t["ccol"][:], scalar1=C52, scalar2=None,
                op0=ALU.add, wait=[tok_c1])
    tok_iota = op("gpsimd", "iota", t["ia"][:, 0:128], pattern=[[1, 128]],
                  base=0, channel_multiplier=-1)
    tok_iden = op("gpsimd", "tensor_scalar", out=t["iden"][:],
                  in0=t["ia"][:, 0:128], scalar1=0, scalar2=None,
                  op0=ALU.is_equal, wait=[tok_iota])
    tok_ss = op("vector", "tensor_tensor", out=t["sssb"][:], in0=t["s0sb"][:],
                in1=t["s1sb"][:], op=ALU.mult, wait=[tok_s0, tok_s1])
    tok_rss = op("vector", "reciprocal", out=t["rsssb"][:], in_=t["sssb"][:],
                 wait=[tok_ss])
    tok_b0 = op("tensor", "matmul", t["pbr"][:, 0:1], t["ones"][:],
                t["s0sb"][:], start=True, stop=True,
                wait=[tok_ones, tok_s0])
    tok_b1 = op("tensor", "matmul", t["pbr"][:, 1:2], t["ones"][:],
                t["s1sb"][:], start=True, stop=True, wait=[tok_s1])
    tok_b2 = op("tensor", "matmul", t["pbr"][:, 2:3], t["ones"][:],
                t["rsssb"][:], start=True, stop=True, wait=[tok_rss])
    tok_sc = op("scalar", "copy", out=t["scol"][:], in_=t["pbr"][:, 0:1],
                wait=[tok_b0])
    tok_sc1 = op("scalar", "copy", out=t["s1col"][:], in_=t["pbr"][:, 1:2],
                 wait=[tok_b1])
    tok_isc = op("scalar", "copy", out=t["iscol"][:], in_=t["pbr"][:, 2:3],
                 wait=[tok_b2])

    st2k_free = [NOTOK, NOTOK]
    pieces = []
    for k in range(KD):
        for p in range(2):
            pieces.append(
                (hb["bt"][128 * k:128 * (k + 1), 512 * p:512 * (p + 1)],
                 t["b_bf"][:, NST * k + 512 * p:NST * k + 512 * (p + 1)]))
    for k in range(KN):
        for p in range(4):
            pieces.append(
                (hb["ct"][128 * k:128 * (k + 1), 512 * p:512 * (p + 1)],
                 t["c_bf"][:, D * k + 512 * p:D * k + 512 * (p + 1)]))
    NG = len(pieces) // 4
    bufs = [[t["ge"][0], t["ge"][2]], [t["ge"][1], t["ge"][3]],
            t["te"], t["oe"]]
    grp_tok = [None] * NG
    grp_cast_last = [None] * NG
    buf_last = [[NOTOK, NOTOK] for _ in range(4)]

    def issue_group(g):
        par = g % 2
        w = [grp_cast_last[g - 2]] if g >= 2 else []
        tk = None
        for i in range(4):
            src, _ = pieces[4 * g + i]
            tk = dma("gpsimd", f"X{par}", bufs[i][par][:], src,
                     wait=w if i == 0 else [])
        grp_tok[g] = tk

    issue_group(0)
    issue_group(1)
    cast_i = [0]

    def emit_bc_casts(n):
        for _ in range(n):
            i = cast_i[0]
            if i >= len(pieces):
                return
            g, j = divmod(i, 4)
            _, dst = pieces[i]
            tok = op("vector", "tensor_copy", out=dst,
                     in_=bufs[j][g % 2][:],
                     wait=[grp_tok[g]] if j == 0 else [])
            grp_cast_last[g] = tok
            buf_last[j][g % 2] = tok
            if j == 3 and g + 2 < NG:
                issue_group(g + 2)
            cast_i[0] = i + 1

    qfree = [tok_iden]
    qtt = [NOTOK]

    def quant_1024(src_tok, dst_ap, E, extra_wait=(), vsrc=None):
        vw = [src_tok] + list(extra_wait)
        vq = t["vq"]
        if vsrc is None:
            vsrc = vq
        ia, ib = t["ia"], t["ib"]
        i1 = op(E["i1"], "tensor_scalar", out=ia[:],
                in0=vsrc[:].bitcast(I32), scalar1=0x7F800000, scalar2=None,
                op0=ALU.bitwise_and, wait=vw + [qfree[0]])
        i2 = op(E["i2"], "tensor_scalar", out=ib[:], in0=ia[:],
                scalar1=0x3C800000, scalar2=-1,
                op0=ALU.max, op1=ALU.mult, wait=[i1])
        dd = op(E["dd"], "tensor_scalar", out=ia[:], in0=ib[:],
                scalar1=0x10800000, scalar2=-1,
                op0=ALU.add, op1=ALU.mult, wait=[i2])
        inv = op(E["inv"], "tensor_scalar", out=ib[:], in0=ib[:],
                 scalar1=0x8F800000, scalar2=None, op0=ALU.add, wait=[dd])
        tt = op(E["tt"], "tensor_tensor", out=vq[:], in0=vsrc[:],
                in1=ib[:].bitcast(F32), op=ALU.mult, wait=vw + [inv])
        qtt[0] = tt
        nn = op(E["nr"], "tensor_scalar", out=vq[:], in0=vq[:],
                scalar1=t["ccol"][:], scalar2=t["ccol"][:],
                op0=ALU.add, op1=ALU.subtract, wait=[tt, tok_cc])
        qq = op(E["q8"], "tensor_tensor", out=dst_ap, in0=vq[:],
                in1=ia[:].bitcast(F32), op=ALU.mult, wait=[nn, dd])
        qfree[0] = qq
        return qq

    WQD = [NOTOK]
    xf_free = [NOTOK, NOTOK]
    me_free = [NOTOK, NOTOK, NOTOK, NOTOK]
    ge_free = [NOTOK] * 4
    te_free = [NOTOK, NOTOK]
    oe_free = [NOTOK, NOTOK]
    state = {
        "rows_guard": [tok_isc],
        "tanh": [NOTOK, NOTOK],
        "m2": [NOTOK] * 4,
        "ff": [NOTOK, NOTOK],
        "state_last_read": NOTOK,
        "xnb_free": [[], []],
        "xn_read": [NOTOK, NOTOK, NOTOK],
        "sbr_read": [NOTOK, NOTOK],
        "q1_free": [NOTOK, NOTOK],
        "rsc_read": [NOTOK, NOTOK],
        "nmc_read": [NOTOK, NOTOK],
        "mv_read": [NOTOK, NOTOK],
        "stats_last": NOTOK,
    }

    def stats(c):
        ve_toks = []
        mv_toks = []
        for tt_i in range(2):
            row0 = T * c + 128 * tt_i
            btoks = []
            for pz in range(2):
                sl = (2 * tt_i + pz) % 2
                tok_l = dma("sync", f"K{sl}", t["st2k"][sl][:],
                            hb["x_tm"][row0:row0 + 128,
                                       1024 * pz:1024 * (pz + 1)],
                            wait=[st2k_free[sl]])
                for q in range(2):
                    g_ = 2 * pz + q
                    btoks.append(op(
                        "vector", "bn_stats",
                        out=t["bnb"][:, 24 * tt_i + 6 * g_:
                                     24 * tt_i + 6 * (g_ + 1)],
                        in_=t["st2k"][sl][:, 512 * q:512 * (q + 1)],
                        wait=[tok_l] + ([state["stats_last"]]
                                        if g_ == 0 else [])))
                st2k_free[sl] = btoks[-1]
            tok_mv = op("vector", "bn_aggr",
                        out=t["mv"][:, 2 * tt_i:2 * (tt_i + 1)],
                        in_=t["bnb"][:, 24 * tt_i:24 * (tt_i + 1)],
                        wait=btoks + [state["mv_read"][tt_i]])
            ve_toks.append(op("vector", "tensor_scalar",
                              out=t["vep"][:, tt_i:tt_i + 1],
                              in0=t["mv"][:, 2 * tt_i + 1:2 * tt_i + 2],
                              scalar1=1e-5, scalar2=None, op0=ALU.add,
                              wait=[tok_mv]))
            mv_toks.append(tok_mv)
        tok_b1 = op("vector", "tensor_scalar", out=t["sit"][:],
                    in0=t["vep"][:].bitcast(I32), scalar1=1, scalar2=None,
                    op0=ALU.logical_shift_right, wait=ve_toks)
        tok_y = op("vector", "tensor_scalar", out=t["sit"][:],
                   in0=t["sit"][:], scalar1=-1, scalar2=0x5F3759DF,
                   op0=ALU.mult, op1=ALU.add, wait=[tok_b1])
        yin = t["sit"][:].bitcast(F32)
        youts = [t["yt"][:], t["wt"][:], t["rsc"][:, 0:2]]
        for it in range(3):
            tok_a = op("vector", "tensor_tensor", out=t["zt"][:],
                       in0=yin, in1=yin, op=ALU.mult, wait=[tok_y])
            tok_b = op("vector", "tensor_tensor", out=t["zt"][:],
                       in0=t["vep"][:], in1=t["zt"][:], op=ALU.mult,
                       wait=[tok_a])
            tok_c3 = op("vector", "tensor_scalar", out=t["zt"][:],
                        in0=t["zt"][:], scalar1=-0.5, scalar2=1.5,
                        op0=ALU.mult, op1=ALU.add, wait=[tok_b])
            w_ = [tok_c3, tok_y]
            if it == 2:
                w_ += list(state["rsc_read"])
            tok_y = op("vector", "tensor_tensor", out=youts[it],
                       in0=yin, in1=t["zt"][:], op=ALU.mult, wait=w_)
            yin = youts[it]
        out = []
        for tt_i in range(2):
            tok_nm = op("vector", "scalar_tensor_tensor",
                        out=t["nmc"][:, tt_i:tt_i + 1],
                        in0=t["mv"][:, 2 * tt_i:2 * tt_i + 1], scalar=-1.0,
                        in1=t["rsc"][:, tt_i:tt_i + 1],
                        op0=ALU.mult, op1=ALU.mult,
                        wait=[mv_toks[tt_i], tok_y,
                              state["nmc_read"][tt_i]])
            state["mv_read"][tt_i] = tok_nm
            out.append((tok_y, tok_nm))
        state["stats_last"] = out[-1][1]
        return out

    def rows_and_bcast(c, mvt):
        tr = []
        for tt_i in range(2):
            tok_tr1 = op("tensor", "transpose",
                         t["pbr"][0:1, 128 * tt_i:128 * (tt_i + 1)],
                         t["rsc"][:, tt_i:tt_i + 1], t["iden"][:],
                         wait=[mvt[tt_i][0], tok_iden]
                         + state["rows_guard"])
            tok_tr2 = op("tensor", "transpose",
                         t["pbn"][0:1, 128 * tt_i:128 * (tt_i + 1)],
                         t["nmc"][:, tt_i:tt_i + 1], t["iden"][:],
                         wait=[mvt[tt_i][1]])
            state["rsc_read"][tt_i] = tok_tr1
            state["nmc_read"][tt_i] = tok_tr2
            tr += [tok_tr1, tok_tr2]
        tok_rr = op("scalar", "activation", out=t["rrow"][:],
                    in_=t["pbr"][0:1, 0:T], func=AF.Copy,
                    wait=tr + state["rows_guard"])
        tok_nr = op("scalar", "activation", out=t["nrow"][:],
                    in_=t["pbn"][0:1, 0:T], func=AF.Copy, wait=tr)
        toks = []
        for half in range(2):
            toks.append(op("tensor", "matmul",
                           t["pbr"][:, T * half:T * (half + 1)],
                           t["ones"][:], t["rrow"][:], start=True, stop=True,
                           wait=[tok_rr, tok_nr] if half == 0 else []))
            toks.append(op("tensor", "matmul",
                           t["pbn"][:, T * half:T * (half + 1)],
                           t["ones"][:], t["nrow"][:], start=True, stop=True,
                           wait=[]))
        par = c % 2
        cptoks = []
        for dst, src in ((t["sbr"][par], t["pbr"]), (t["sbn"][par],
                                                    t["pbn"])):
            for hh in range(2):
                w_ = []
                if not cptoks:
                    w_ = [toks[-1], state["sbr_read"][par]]
                cptoks.append(op("scalar", "activation",
                                 out=dst[:, EW * hh:EW * (hh + 1)],
                                 in_=src[:], func=AF.Copy, wait=w_))
        state["rows_guard"] = [cptoks[-1]]
        return cptoks

    def etile_pair(c, p, bc_toks, xslot=None):
        par = c % 2
        sl = p % 2
        xs = p % 3 if xslot is None else xslot
        ltok = None
        for q in range(4):
            fi = 4 * p + q
            ltok = dma(
                "sync", f"F{sl}", t["xf"][sl][:, T * q:T * (q + 1)],
                hb["x_fm"][128 * fi:128 * (fi + 1), T * c:T * (c + 1)],
                wait=[xf_free[sl]] if q == 0 else [])
        tok_t0 = op("vector", "tensor_tensor", out=t["vq"][:],
                    in0=t["xf"][sl][:], in1=t["sbr"][par][:],
                    op=ALU.mult, wait=[ltok, qfree[0]] + bc_toks)
        xf_free[sl] = tok_t0
        tok_xn = op("vector", "tensor_tensor", out=t["xn"][xs][:],
                    in0=t["vq"][:], in1=t["sbn"][par][:],
                    op=ALU.add, wait=[tok_t0, state["xn_read"][xs]])
        tok_v = op("vector", "tensor_scalar", out=t["vq"][:],
                   in0=t["xn"][xs][:], scalar1=t["scol"][:], scalar2=None,
                   op0=ALU.mult, wait=[tok_xn, tok_sc])
        return tok_xn, tok_v, xs

    def emit_xb_pair(A, p):
        cp = A["c"] % 2
        xnb = t["xnb0"] if cp == 0 else t["xnb1"]
        p_, xt, xs = A["xn_toks"][p]
        tok_xb = op("scalar", "activation",
                    out=xnb[:, 2 * EW * p_:2 * EW * (p_ + 1)],
                    in_=t["xn"][xs][:], func=AF.Copy,
                    wait=[xt] + (state["xnb_free"][cp] if p_ == 0 else []))
        state["xn_read"][xs] = tok_xb
        A["xnb_toks"].append(tok_xb)

    def mm1(c, xnb_last, jps, st_toks):
        cp = c % 2
        xnb = t["xnb0"] if cp == 0 else t["xnb1"]
        for jp in jps:
            bank = t["pm1"][jp % 2]
            grp = []
            for h in range(2):
                j = 2 * jp + h
                for k in range(KD):
                    w_ = []
                    if k == 0 and h == 0:
                        w_ = [xnb_last, state["tanh"][jp % 2],
                              state["m2"][2 + (jp % 2)]]
                    tok = op("tensor", "matmul",
                             bank[:, T * h:T * (h + 1)],
                             t["b_bf"][:, NST * k + 128 * j:
                                       NST * k + 128 * (j + 1)],
                             xnb[:, T * k:T * (k + 1)],
                             start=(k == 0), stop=(k == KD - 1),
                             wait=w_, inc="E" if k == KD - 1 else False)
                    if k == KD - 1:
                        grp.append(tok)
            tok_th = op("scalar", "activation",
                        out=t["state"][:, EW * jp:EW * (jp + 1)],
                        in_=bank[:], func=AF.Tanh,
                        wait=grp + [state["state_last_read"]])
            state["tanh"][jp % 2] = tok_th
            st_toks.append(tok_th)

    def bpair(c, jp, st_toks, q1_last, A, pair=None, xbp=None,
              xnb_last=NOTOK):
        cp = c % 2
        xnb = t["xnb0"] if cp == 0 else t["xnb1"]
        q1 = t["q10"] if cp == 0 else t["q11"]
        ms = jp % 4
        bank2 = (t["pm2"] + t["pm1"])[ms]
        bankf = t["pff"][jp % 2]
        g2 = []
        for h in range(2):
            j = 2 * jp + h
            for k in range(KN):
                w_ = []
                if k == 0 and h == 0:
                    w_ = st_toks + [state["m2"][ms]]
                tok = op("tensor", "matmul",
                         bank2[:, T * h:T * (h + 1)],
                         t["c_bf"][:, D * k + 128 * j:D * k + 128 * (j + 1)],
                         t["state"][:, T * k:T * (k + 1)],
                         start=(k == 0), stop=(k == KN - 1),
                         wait=w_, inc="E" if k == KN - 1 else False)
                if k == KN - 1:
                    g2.append(tok)
        gf = []
        for h in range(2):
            j = 2 * jp + h
            for k in range(KD):
                w_ = []
                if k == 0 and h == 0:
                    w_ = [q1_last, state["ff"][jp % 2], WQD[0]]
                tok = op("tensor", "matmul",
                         bankf[:, T * h:T * (h + 1)],
                         t["wq"][:, D * k + 128 * j:D * k + 128 * (j + 1)],
                         q1[:, T * k:T * (k + 1)],
                         start=(k == 0), stop=(k == KD - 1),
                         wait=w_, inc="E" if k == KD - 1 else False)
                if k == KD - 1:
                    gf.append(tok)
        es = jp % 2
        tok_cp = op("scalar", "activation", out=t["me"][ms][:],
                    in_=bank2[:], func=AF.Copy,
                    wait=g2 + [me_free[ms]])
        state["m2"][ms] = tok_cp
        tok_g = op("scalar", "activation", out=t["ge"][ms][:],
                   in_=bankf[:], func=AF.Silu, scale=t["iscol"][:],
                   wait=gf + [tok_isc, ge_free[ms]])
        state["ff"][jp % 2] = tok_g
        if A is not None and pair is not None:
            xn_, v_, xs_ = etile_pair(A["c"], pair, A["bc"])
            A["xn_last"] = xn_
            A["xn_toks"].append((pair, xn_, xs_))
            q1n = t["q10"] if A["c"] % 2 == 0 else t["q11"]
            A["q_toks"].append(quant_1024(
                v_, q1n[:, 1024 * pair:1024 * (pair + 1)], QENG_R,
                extra_wait=[state["q1_free"][A["c"] % 2]]))
        if xbp is not None:
            emit_xb_pair(A, xbp)
        stoks = []
        for h in range(2):
            j = 2 * jp + h
            sm = op("gpsimd", "tensor_scalar",
                    out=t["te"][es][:, T * h:T * (h + 1)],
                    in0=xnb[:, EW * jp + T * h:EW * jp + T * (h + 1)],
                    scalar1=t["dcol"][:, j:j + 1], scalar2=None,
                    op0=ALU.mult,
                    wait=([dtoks[-1], te_free[es], xnb_last]
                          if h == 0 else []))
            stoks.append(op("gpsimd", "tensor_tensor",
                            out=t["me"][ms][:, T * h:T * (h + 1)],
                            in0=t["te"][es][:, T * h:T * (h + 1)],
                            in1=t["me"][ms][:, T * h:T * (h + 1)],
                            op=ALU.add, wait=[sm, tok_cp]))
        tok_t = op("gpsimd", "tensor_tensor", out=t["te"][es][:],
                   in0=t["me"][ms][:], in1=t["ge"][ms][:],
                   op=ALU.mult, wait=stoks + [tok_g])
        me_free[ms] = tok_t
        ge_free[ms] = tok_t
        l2 = []
        for h in range(2):
            fi = 2 * jp + h
            l2.append(dma(
                "gpsimd", f"X{es}", t["oe"][es][:, T * h:T * (h + 1)],
                hb["x_fm"][128 * fi:128 * (fi + 1), T * c:T * (c + 1)],
                wait=[oe_free[es]] if h == 0 else []))
        l2 = [l2[-1]]
        tok_o = op("gpsimd", "tensor_tensor", out=t["oe"][es][:],
                   in0=t["oe"][es][:], in1=t["te"][es][:],
                   op=ALU.add, wait=l2 + [tok_t])
        te_free[es] = tok_o
        otoks = []
        for h in range(2):
            fi = 2 * jp + h
            otoks.append(dma(
                "gpsimd", f"P{es}", hb["out"][128 * fi:128 * (fi + 1),
                                              T * c:T * (c + 1)],
                t["oe"][es][:, T * h:T * (h + 1)],
                wait=[tok_o] if h == 0 else []))
        oe_free[es] = otoks[-1]

    A0 = {"c": 0, "bc": None, "xn_last": None, "xn_toks": [],
          "xnb_toks": [], "q_toks": []}
    A1 = {"c": 1, "bc": None, "xn_last": None, "xn_toks": [],
          "xnb_toks": [], "q_toks": []}
    mv0 = mv1 = None
    vfree = [NOTOK, NOTOK]
    a0_at = {6: 0, 12: 1, 18: 2, 24: 3}
    xb_at = {9: 0, 15: 1, 21: 2, 27: 3}
    i_piece = 0
    for k in range(KD):
        for h in range(2):
            sl = (2 * k + h) % 2
            ltok = dma("sync", f"K{sl}", t["st2k"][sl][:],
                       hb["w"][128 * k:128 * (k + 1),
                               1024 * h:1024 * (h + 1)],
                       wait=[st2k_free[sl]])
            tok_v = op("scalar", "activation", out=t["xn"][sl][:],
                       in_=t["st2k"][sl][:], func=AF.Copy,
                       scale=t["s1col"][:],
                       wait=[ltok, tok_sc1, vfree[sl]])
            st2k_free[sl] = tok_v
            quant_1024(tok_v,
                       t["wq"][:, D * k + 1024 * h: D * k + 1024 * (h + 1)],
                       QENG_W, vsrc=t["xn"][sl])
            vfree[sl] = qtt[0]
            emit_bc_casts(2)
            i_piece += 1
            if i_piece == 2:
                mv0 = stats(0)
            elif i_piece == 4:
                A0["bc"] = rows_and_bcast(0, mv0)
            elif i_piece in a0_at:
                p = a0_at[i_piece]
                xn_, v_, xs_ = etile_pair(0, p, A0["bc"], xslot=2)
                A0["xn_last"] = xn_
                A0["xn_toks"].append((p, xn_, xs_))
                A0["q_toks"].append(quant_1024(
                    v_, t["q10"][:, 1024 * p:1024 * (p + 1)],
                    QENG_R, extra_wait=[state["q1_free"][0]]))
            elif i_piece in xb_at:
                emit_xb_pair(A0, xb_at[i_piece])
            elif i_piece == 30:
                mv1 = stats(1)
    WQD[0] = qfree[0]
    state["xn_read"][0] = vfree[0]
    state["xn_read"][1] = vfree[1]
    emit_bc_casts(len(pieces))
    ge_free[:] = [buf_last[0][0], buf_last[1][0],
                  buf_last[0][1], buf_last[1][1]]
    te_free[:] = list(buf_last[2])
    oe_free[:] = list(buf_last[3])
    waitonly("gpsimd", ("X0", cnt["X0"]), ("X1", cnt["X1"]))
    state["sbr_read"][0] = A0["xn_last"]
    A1["bc"] = rows_and_bcast(1, mv1)

    Acur, Anext = A0, A1
    for c in range(NCH):
        if c + 2 < NCH:
            Anext2 = {"c": c + 2, "bc": None, "xn_last": None,
                      "xn_toks": [], "xnb_toks": [], "q_toks": []}
            mvt2 = stats(c + 2)
        else:
            Anext2, mvt2 = None, None
        st_toks = []
        mm1(c, Acur["xnb_toks"][-1], (0, 1, 2, 3), st_toks)
        if Anext2 is not None:
            Anext2["bc"] = rows_and_bcast(c + 2, mvt2)
        for jp in range(KD // 2):
            pair = jp if (jp < 4 and Anext is not None) else None
            xbp = {0: 0, 4: 1, 5: 2, 6: 3}.get(jp) \
                if Anext is not None else None
            bpair(c, jp, st_toks, Acur["q_toks"][-1], Anext, pair, xbp,
                  xnb_last=Acur["xnb_toks"][-1])
        if Anext is not None:
            state["sbr_read"][(c + 1) % 2] = Anext["xn_last"]
        state["state_last_read"] = ("E", cnt["E"])
        state["xnb_free"][c % 2] = [("E", cnt["E"]), ("V", cnt["V"])]
        state["q1_free"][c % 2] = ("E", cnt["E"])
        Acur, Anext = Anext, Anext2

    waitonly("sync", *[(s, cnt[s]) for s in
                       ("SC", "CN", "K0", "K1", "F0", "F1")])
    waitonly("gpsimd", ("P0", cnt["P0"]), ("P1", cnt["P1"]),
             ("X0", cnt["X0"]), ("X1", cnt["X1"]))


_NC_CACHE = []


def kernel(L0_attn_x, L0_attn_B, L0_attn_C, L0_attn_D,
           L0_ffn_input2, L0_ffn_scale, L0_ffn_scale1):
    x = np.asarray(L0_attn_x, dtype=np.float32).reshape(NCORES * TOK, D)
    bt = np.ascontiguousarray(np.asarray(L0_attn_B, dtype=np.float32).T)
    ct = np.ascontiguousarray(np.asarray(L0_attn_C, dtype=np.float32).T)
    w = np.ascontiguousarray(np.asarray(L0_ffn_input2, dtype=np.float32))
    dv = np.asarray(L0_attn_D, dtype=np.float32).reshape(D, 1)
    s0 = np.asarray(L0_ffn_scale, dtype=np.float32).reshape(1, 1)
    s1 = np.asarray(L0_ffn_scale1, dtype=np.float32).reshape(1, 1)

    in_maps = []
    for i in range(NCORES):
        sl = x[TOK * i:TOK * (i + 1)]
        in_maps.append({
            "x_tm": np.ascontiguousarray(sl),
            "x_fm": np.ascontiguousarray(sl.T),
            "bt": bt, "ct": ct, "w": w, "dv": dv, "s0": s0, "s1": s1,
        })
    if not _NC_CACHE:
        _NC_CACHE.append(build_nc())
    nc = _NC_CACHE[0]
    res = run_bass_kernel_spmd(nc, in_maps, core_ids=list(range(NCORES)))
    outs = [np.asarray(r["out"]).T for r in res.results]
    return np.ascontiguousarray(
        np.concatenate(outs, axis=0).reshape(4, 4096, D)).astype(
            np.float32, copy=False)


# version 32
# speedup vs baseline: 1.6826x; 1.6826x over previous
import numpy as np
from contextlib import ExitStack

import concourse.bass as bass
import concourse.mybir as mybir
from concourse.bass_utils import run_bass_kernel_spmd

F32 = mybir.dt.float32
BF16 = mybir.dt.bfloat16
FP8 = mybir.dt.float8e4
I32 = mybir.dt.int32
AF = mybir.ActivationFunctionType
ALU = mybir.AluOpType

NCORES = 8
D = 2048
NST = 1024
TOK = 2048
T = 256
NCH = TOK // T
KD = D // 128
KN = NST // 128
ET = 8
EW = 512

ENG_SEM = {"sync": "SD", "vector": "V", "scalar": "A",
           "gpsimd": "P", "tensor": "E"}

C52 = float(3 * 2 ** 52)
C82 = float(3 * 2 ** 81)

QENG_W = {"i1": "vector", "i2": "vector", "inv": "vector", "dd": "vector",
          "tt": "vector", "nr": "vector", "dif": "vector",
          "fin": "vector", "q8": "vector"}
QENG_R = {"i1": "vector", "i2": "vector", "inv": "vector", "dd": "vector",
          "tt": "vector", "nr": "vector", "dif": "vector",
          "fin": "vector", "q8": "vector"}


def build_nc():
    nc = bass.Bass("TRN2", target_bir_lowering=False)
    hb = {}
    hb["x_tm"] = nc.dram_tensor("x_tm", [TOK, D], F32, kind="ExternalInput").ap()
    hb["x_fm"] = nc.dram_tensor("x_fm", [D, TOK], F32, kind="ExternalInput").ap()
    hb["bt"] = nc.dram_tensor("bt", [D, NST], F32, kind="ExternalInput").ap()
    hb["ct"] = nc.dram_tensor("ct", [NST, D], F32, kind="ExternalInput").ap()
    hb["w"] = nc.dram_tensor("w", [D, D], F32, kind="ExternalInput").ap()
    hb["dv"] = nc.dram_tensor("dv", [D, 1], F32, kind="ExternalInput").ap()
    hb["s0"] = nc.dram_tensor("s0", [1, 1], F32, kind="ExternalInput").ap()
    hb["s1"] = nc.dram_tensor("s1", [1, 1], F32, kind="ExternalInput").ap()
    hb["out"] = nc.dram_tensor("out", [D, TOK], F32, kind="ExternalOutput").ap()

    with ExitStack() as st:
        sb = lambda n, s, d: st.enter_context(nc.sbuf_tensor(n, s, d))
        ps = lambda n, s: st.enter_context(nc.psum_tensor(n, s, F32))
        t = {}
        t["b_bf"] = sb("b_bf", [128, KD * NST], BF16)
        t["c_bf"] = sb("c_bf", [128, KN * D], BF16)
        t["wq"] = sb("wq", [128, KD * D], FP8)
        t["xnb0"] = sb("xnb0", [128, KD * T], BF16)
        t["xnb1"] = sb("xnb1", [128, KD * T], BF16)
        t["q10"] = sb("q10", [128, KD * T], FP8)
        t["q11"] = sb("q11", [128, KD * T], FP8)
        t["state"] = sb("state", [128, KN * T], BF16)
        t["st2k"] = [sb(f"st2k{i}", [128, 2 * EW], F32) for i in range(2)]
        t["xf"] = [sb(f"xf{i}", [128, 2 * EW], F32) for i in range(2)]
        t["vq"] = sb("vq", [128, 2 * EW], F32)
        t["ia"] = sb("ia", [128, 2 * EW], I32)
        t["ib"] = sb("ib", [128, 2 * EW], I32)
        t["xn"] = [sb(f"xn{i}", [128, 2 * EW], F32) for i in range(3)]
        t["sbr"] = [sb(f"sbr{i}", [128, 2 * EW], F32) for i in range(2)]
        t["sbn"] = [sb(f"sbn{i}", [128, 2 * EW], F32) for i in range(2)]
        t["me"] = [sb(f"me{i}", [128, EW], F32) for i in range(4)]
        t["ge"] = [sb(f"ge{i}", [128, EW], F32) for i in range(4)]
        t["te"] = [sb(f"te{i}", [128, EW], F32) for i in range(2)]
        t["oe"] = [sb(f"oe{i}", [128, EW], F32) for i in range(2)]
        t["bnb"] = sb("bnb", [128, 48], F32)
        t["mv"] = sb("mv", [128, 4], F32)
        t["vep"] = sb("vep", [128, 2], F32)
        t["rsc"] = sb("rsc", [128, 2], F32)
        t["nmc"] = sb("nmc", [128, 2], F32)
        t["yt"] = sb("yt", [128, 2], F32)
        t["wt"] = sb("wt", [128, 2], F32)
        t["zt"] = sb("zt", [128, 2], F32)
        t["sit"] = sb("sit", [128, 2], I32)
        t["rrow"] = sb("rrow", [1, T], F32)
        t["nrow"] = sb("nrow", [1, T], F32)
        t["ones"] = sb("ones", [1, 128], F32)
        t["iden"] = sb("iden", [128, 128], F32)
        t["s0sb"] = sb("s0sb", [1, 1], F32)
        t["s1sb"] = sb("s1sb", [1, 1], F32)
        t["sssb"] = sb("sssb", [1, 1], F32)
        t["rsssb"] = sb("rsssb", [1, 1], F32)
        t["scol"] = sb("scol", [128, 1], F32)
        t["dcolD"] = sb("dcolD", [128, 1], F32)
        t["ccol"] = sb("ccol", [128, 1], F32)
        t["s1col"] = sb("s1col", [128, 1], F32)
        t["iscol"] = sb("iscol", [128, 1], F32)
        t["dcol"] = sb("dcol", [128, KD], F32)
        t["pm1"] = [ps("pm1a", [128, EW]), ps("pm1b", [128, EW])]
        t["pm2"] = [ps("pm2a", [128, EW]), ps("pm2b", [128, EW])]
        t["pff"] = [ps("pffa", [128, EW]), ps("pffb", [128, EW])]
        t["pbr"] = ps("pbr", [128, EW])
        t["pbn"] = ps("pbn", [128, EW])

        sems = {}
        for name in ("SC", "CN", "K0", "K1", "F0", "F1", "X0", "X1",
                     "P0", "P1", "V", "A", "P", "E"):
            sems[name] = st.enter_context(nc.semaphore(name))

        blk = st.enter_context(nc.Block())
        for eng in ("sync", "vector", "scalar", "gpsimd", "tensor"):
            getattr(blk, eng)(lambda e, eng=eng: emit(nc, t, hb, sems, eng))
    return nc


def emit(nc, t, hb, sems, sel):
    cnt = {k: 0 for k in sems}

    def op(eng, meth, *a, wait=(), inc=None, inck=1, **kw):
        s = inc if inc is not None else ENG_SEM[eng]
        if eng == sel:
            e = getattr(nc, eng)
            for sname, v in wait:
                if v > 0:
                    e.wait_ge(sems[sname], v)
            r = getattr(e, meth)(*a, **kw)
            if s is not False:
                r.then_inc(sems[s], inck)
        if s is False:
            return None
        cnt[s] += inck
        return (s, cnt[s])

    def waitonly(eng, *toks):
        if eng == sel:
            e = getattr(nc, eng)
            for sname, v in toks:
                if v > 0:
                    e.wait_ge(sems[sname], v)

    def dma(eng, sem, dst, src, wait=()):
        return op(eng, "dma_start", wait=wait, inc=sem, inck=16,
                  out=dst, in_=src)

    NOTOK = ("V", 0)

    dma("sync", "SC", t["s0sb"][:], hb["s0"][:])
    tok_s1 = dma("sync", "SC", t["s1sb"][:], hb["s1"][:])
    tok_s0 = tok_s1
    dtoks = []
    for k in range(KD):
        dtoks.append(dma("sync", "CN", t["dcol"][:, k:k + 1],
                         hb["dv"][128 * k:128 * (k + 1), :]))
    tok_ones = op("vector", "memset", t["ones"][:], 1.0)
    tok_d1 = op("vector", "memset", t["dcolD"][:], 1.0)
    tok_d2 = op("vector", "tensor_scalar", out=t["dcolD"][:],
                in0=t["dcolD"][:], scalar1=float(2.0 ** -30), scalar2=1.0,
                op0=ALU.add, op1=ALU.subtract, wait=[tok_d1])
    tok_dD = op("vector", "tensor_scalar", out=t["dcolD"][:],
                in0=t["dcolD"][:], scalar1=float(2.0 ** 30), scalar2=None,
                op0=ALU.mult, wait=[tok_d2])
    tok_c1 = op("vector", "tensor_scalar", out=t["ccol"][:],
                in0=t["dcolD"][:], scalar1=float(C82 - C52), scalar2=None,
                op0=ALU.mult, wait=[tok_dD])
    tok_cc = op("vector", "tensor_scalar", out=t["ccol"][:],
                in0=t["ccol"][:], scalar1=C52, scalar2=None,
                op0=ALU.add, wait=[tok_c1])
    tok_iota = op("gpsimd", "iota", t["ia"][:, 0:128], pattern=[[1, 128]],
                  base=0, channel_multiplier=-1)
    tok_iden = op("gpsimd", "tensor_scalar", out=t["iden"][:],
                  in0=t["ia"][:, 0:128], scalar1=0, scalar2=None,
                  op0=ALU.is_equal, wait=[tok_iota])
    tok_ss = op("vector", "tensor_tensor", out=t["sssb"][:], in0=t["s0sb"][:],
                in1=t["s1sb"][:], op=ALU.mult, wait=[tok_s0, tok_s1])
    tok_rss = op("vector", "reciprocal", out=t["rsssb"][:], in_=t["sssb"][:],
                 wait=[tok_ss])
    tok_b0 = op("tensor", "matmul", t["pbr"][:, 0:1], t["ones"][:],
                t["s0sb"][:], start=True, stop=True,
                wait=[tok_ones, tok_s0])
    tok_b1 = op("tensor", "matmul", t["pbr"][:, 1:2], t["ones"][:],
                t["s1sb"][:], start=True, stop=True, wait=[tok_s1])
    tok_b2 = op("tensor", "matmul", t["pbr"][:, 2:3], t["ones"][:],
                t["rsssb"][:], start=True, stop=True, wait=[tok_rss])
    tok_sc = op("scalar", "copy", out=t["scol"][:], in_=t["pbr"][:, 0:1],
                wait=[tok_b0])
    tok_sc1 = op("scalar", "copy", out=t["s1col"][:], in_=t["pbr"][:, 1:2],
                 wait=[tok_b1])
    tok_isc = op("scalar", "copy", out=t["iscol"][:], in_=t["pbr"][:, 2:3],
                 wait=[tok_b2])

    st2k_free = [NOTOK, NOTOK]
    pieces = []
    for k in range(KD):
        for p in range(2):
            pieces.append(
                (hb["bt"][128 * k:128 * (k + 1), 512 * p:512 * (p + 1)],
                 t["b_bf"][:, NST * k + 512 * p:NST * k + 512 * (p + 1)]))
    for k in range(KN):
        for p in range(4):
            pieces.append(
                (hb["ct"][128 * k:128 * (k + 1), 512 * p:512 * (p + 1)],
                 t["c_bf"][:, D * k + 512 * p:D * k + 512 * (p + 1)]))
    NG = len(pieces) // 4
    bufs = [[t["ge"][0], t["ge"][2]], [t["ge"][1], t["ge"][3]],
            t["te"], t["oe"]]
    grp_tok = [None] * NG
    grp_cast_last = [None] * NG
    buf_last = [[NOTOK, NOTOK] for _ in range(4)]

    def issue_group(g):
        par = g % 2
        w = [grp_cast_last[g - 2]] if g >= 2 else []
        tk = None
        for i in range(4):
            src, _ = pieces[4 * g + i]
            tk = dma("gpsimd", f"X{par}", bufs[i][par][:], src,
                     wait=w if i == 0 else [])
        grp_tok[g] = tk

    issue_group(0)
    issue_group(1)
    cast_i = [0]

    def emit_bc_casts(n):
        for _ in range(n):
            i = cast_i[0]
            if i >= len(pieces):
                return
            g, j = divmod(i, 4)
            _, dst = pieces[i]
            tok = op("vector", "tensor_copy", out=dst,
                     in_=bufs[j][g % 2][:],
                     wait=[grp_tok[g]] if j == 0 else [])
            grp_cast_last[g] = tok
            buf_last[j][g % 2] = tok
            if j == 3 and g + 2 < NG:
                issue_group(g + 2)
            cast_i[0] = i + 1

    qfree = [tok_iden]
    qtt = [NOTOK]

    def quant_1024(src_tok, dst_ap, E, extra_wait=(), vsrc=None):
        vw = [src_tok] + list(extra_wait)
        vq = t["vq"]
        if vsrc is None:
            vsrc = vq
        ia, ib = t["ia"], t["ib"]
        i1 = op(E["i1"], "tensor_scalar", out=ia[:],
                in0=vsrc[:].bitcast(I32), scalar1=0x7F800000, scalar2=None,
                op0=ALU.bitwise_and, wait=vw + [qfree[0]])
        dd = op(E["dd"], "tensor_scalar", out=ia[:], in0=ia[:],
                scalar1=0x3C800000, scalar2=0x10800000,
                op0=ALU.max, op1=ALU.subtract, wait=[i1])
        inv = op(E["inv"], "tensor_scalar", out=ib[:], in0=ia[:],
                 scalar1=-1, scalar2=0x7F000000,
                 op0=ALU.mult, op1=ALU.add, wait=[dd])
        tt = op(E["tt"], "tensor_tensor", out=vq[:], in0=vsrc[:],
                in1=ib[:].bitcast(F32), op=ALU.mult, wait=vw + [inv])
        qtt[0] = tt
        nn = op(E["nr"], "tensor_scalar", out=vq[:], in0=vq[:],
                scalar1=t["ccol"][:], scalar2=t["ccol"][:],
                op0=ALU.add, op1=ALU.subtract, wait=[tt, tok_cc])
        qq = op(E["q8"], "tensor_tensor", out=dst_ap, in0=vq[:],
                in1=ia[:].bitcast(F32), op=ALU.mult, wait=[nn, dd])
        qfree[0] = qq
        return qq

    WQD = [NOTOK]
    xf_free = [NOTOK, NOTOK]
    me_free = [NOTOK, NOTOK, NOTOK, NOTOK]
    ge_free = [NOTOK] * 4
    te_free = [NOTOK, NOTOK]
    oe_free = [NOTOK, NOTOK]
    state = {
        "rows_guard": [tok_isc],
        "tanh": [NOTOK, NOTOK],
        "m2": [NOTOK] * 4,
        "ff": [NOTOK, NOTOK],
        "state_last_read": NOTOK,
        "xnb_free": [[], []],
        "xn_read": [NOTOK, NOTOK, NOTOK],
        "sbr_read": [NOTOK, NOTOK],
        "q1_free": [NOTOK, NOTOK],
        "rsc_read": [NOTOK, NOTOK],
        "nmc_read": [NOTOK, NOTOK],
        "mv_read": [NOTOK, NOTOK],
        "stats_last": NOTOK,
    }

    def stats(c):
        ve_toks = []
        mv_toks = []
        for tt_i in range(2):
            row0 = T * c + 128 * tt_i
            btoks = []
            for pz in range(2):
                sl = (2 * tt_i + pz) % 2
                tok_l = dma("sync", f"K{sl}", t["st2k"][sl][:],
                            hb["x_tm"][row0:row0 + 128,
                                       1024 * pz:1024 * (pz + 1)],
                            wait=[st2k_free[sl]])
                for q in range(2):
                    g_ = 2 * pz + q
                    btoks.append(op(
                        "vector", "bn_stats",
                        out=t["bnb"][:, 24 * tt_i + 6 * g_:
                                     24 * tt_i + 6 * (g_ + 1)],
                        in_=t["st2k"][sl][:, 512 * q:512 * (q + 1)],
                        wait=[tok_l] + ([state["stats_last"]]
                                        if g_ == 0 else [])))
                st2k_free[sl] = btoks[-1]
            tok_mv = op("vector", "bn_aggr",
                        out=t["mv"][:, 2 * tt_i:2 * (tt_i + 1)],
                        in_=t["bnb"][:, 24 * tt_i:24 * (tt_i + 1)],
                        wait=btoks + [state["mv_read"][tt_i]])
            ve_toks.append(op("vector", "tensor_scalar",
                              out=t["vep"][:, tt_i:tt_i + 1],
                              in0=t["mv"][:, 2 * tt_i + 1:2 * tt_i + 2],
                              scalar1=1e-5, scalar2=None, op0=ALU.add,
                              wait=[tok_mv]))
            mv_toks.append(tok_mv)
        tok_b1 = op("vector", "tensor_scalar", out=t["sit"][:],
                    in0=t["vep"][:].bitcast(I32), scalar1=1, scalar2=None,
                    op0=ALU.logical_shift_right, wait=ve_toks)
        tok_y = op("vector", "tensor_scalar", out=t["sit"][:],
                   in0=t["sit"][:], scalar1=-1, scalar2=0x5F3759DF,
                   op0=ALU.mult, op1=ALU.add, wait=[tok_b1])
        yin = t["sit"][:].bitcast(F32)
        youts = [t["yt"][:], t["wt"][:], t["rsc"][:, 0:2]]
        for it in range(3):
            tok_a = op("vector", "tensor_tensor", out=t["zt"][:],
                       in0=yin, in1=yin, op=ALU.mult, wait=[tok_y])
            tok_b = op("vector", "tensor_tensor", out=t["zt"][:],
                       in0=t["vep"][:], in1=t["zt"][:], op=ALU.mult,
                       wait=[tok_a])
            tok_c3 = op("vector", "tensor_scalar", out=t["zt"][:],
                        in0=t["zt"][:], scalar1=-0.5, scalar2=1.5,
                        op0=ALU.mult, op1=ALU.add, wait=[tok_b])
            w_ = [tok_c3, tok_y]
            if it == 2:
                w_ += list(state["rsc_read"])
            tok_y = op("vector", "tensor_tensor", out=youts[it],
                       in0=yin, in1=t["zt"][:], op=ALU.mult, wait=w_)
            yin = youts[it]
        out = []
        for tt_i in range(2):
            tok_nm = op("vector", "scalar_tensor_tensor",
                        out=t["nmc"][:, tt_i:tt_i + 1],
                        in0=t["mv"][:, 2 * tt_i:2 * tt_i + 1], scalar=-1.0,
                        in1=t["rsc"][:, tt_i:tt_i + 1],
                        op0=ALU.mult, op1=ALU.mult,
                        wait=[mv_toks[tt_i], tok_y,
                              state["nmc_read"][tt_i]])
            state["mv_read"][tt_i] = tok_nm
            out.append((tok_y, tok_nm))
        state["stats_last"] = out[-1][1]
        return out

    def rows_and_bcast(c, mvt):
        tr = []
        for tt_i in range(2):
            tok_tr1 = op("tensor", "transpose",
                         t["pbr"][0:1, 128 * tt_i:128 * (tt_i + 1)],
                         t["rsc"][:, tt_i:tt_i + 1], t["iden"][:],
                         wait=[mvt[tt_i][0], tok_iden]
                         + state["rows_guard"])
            tok_tr2 = op("tensor", "transpose",
                         t["pbn"][0:1, 128 * tt_i:128 * (tt_i + 1)],
                         t["nmc"][:, tt_i:tt_i + 1], t["iden"][:],
                         wait=[mvt[tt_i][1]])
            state["rsc_read"][tt_i] = tok_tr1
            state["nmc_read"][tt_i] = tok_tr2
            tr += [tok_tr1, tok_tr2]
        tok_rr = op("scalar", "activation", out=t["rrow"][:],
                    in_=t["pbr"][0:1, 0:T], func=AF.Copy,
                    wait=tr + state["rows_guard"])
        tok_nr = op("scalar", "activation", out=t["nrow"][:],
                    in_=t["pbn"][0:1, 0:T], func=AF.Copy, wait=tr)
        toks = []
        for half in range(2):
            toks.append(op("tensor", "matmul",
                           t["pbr"][:, T * half:T * (half + 1)],
                           t["ones"][:], t["rrow"][:], start=True, stop=True,
                           wait=[tok_rr, tok_nr] if half == 0 else []))
            toks.append(op("tensor", "matmul",
                           t["pbn"][:, T * half:T * (half + 1)],
                           t["ones"][:], t["nrow"][:], start=True, stop=True,
                           wait=[]))
        par = c % 2
        cptoks = []
        for dst, src in ((t["sbr"][par], t["pbr"]), (t["sbn"][par],
                                                    t["pbn"])):
            for hh in range(2):
                w_ = []
                if not cptoks:
                    w_ = [toks[-1], state["sbr_read"][par]]
                cptoks.append(op("scalar", "activation",
                                 out=dst[:, EW * hh:EW * (hh + 1)],
                                 in_=src[:], func=AF.Copy, wait=w_))
        state["rows_guard"] = [cptoks[-1]]
        return cptoks

    def etile_pair(c, p, bc_toks, xslot=None):
        par = c % 2
        sl = p % 2
        xs = p % 3 if xslot is None else xslot
        ltok = None
        for q in range(4):
            fi = 4 * p + q
            ltok = dma(
                "sync", f"F{sl}", t["xf"][sl][:, T * q:T * (q + 1)],
                hb["x_fm"][128 * fi:128 * (fi + 1), T * c:T * (c + 1)],
                wait=[xf_free[sl]] if q == 0 else [])
        tok_t0 = op("vector", "tensor_tensor", out=t["vq"][:],
                    in0=t["xf"][sl][:], in1=t["sbr"][par][:],
                    op=ALU.mult, wait=[ltok, qfree[0]] + bc_toks)
        xf_free[sl] = tok_t0
        tok_xn = op("vector", "tensor_tensor", out=t["xn"][xs][:],
                    in0=t["vq"][:], in1=t["sbn"][par][:],
                    op=ALU.add, wait=[tok_t0, state["xn_read"][xs]])
        tok_v = op("vector", "tensor_scalar", out=t["vq"][:],
                   in0=t["xn"][xs][:], scalar1=t["scol"][:], scalar2=None,
                   op0=ALU.mult, wait=[tok_xn, tok_sc])
        return tok_xn, tok_v, xs

    def emit_xb_pair(A, p):
        cp = A["c"] % 2
        xnb = t["xnb0"] if cp == 0 else t["xnb1"]
        p_, xt, xs = A["xn_toks"][p]
        tok_xb = op("scalar", "activation",
                    out=xnb[:, 2 * EW * p_:2 * EW * (p_ + 1)],
                    in_=t["xn"][xs][:], func=AF.Copy,
                    wait=[xt] + (state["xnb_free"][cp] if p_ == 0 else []))
        state["xn_read"][xs] = tok_xb
        A["xnb_toks"].append(tok_xb)

    def mm1(c, xnb_last, jps, st_toks):
        cp = c % 2
        xnb = t["xnb0"] if cp == 0 else t["xnb1"]
        for jp in jps:
            bank = t["pm1"][jp % 2]
            grp = []
            for h in range(2):
                j = 2 * jp + h
                for k in range(KD):
                    w_ = []
                    if k == 0 and h == 0:
                        w_ = [xnb_last, state["tanh"][jp % 2],
                              state["m2"][2 + (jp % 2)]]
                    tok = op("tensor", "matmul",
                             bank[:, T * h:T * (h + 1)],
                             t["b_bf"][:, NST * k + 128 * j:
                                       NST * k + 128 * (j + 1)],
                             xnb[:, T * k:T * (k + 1)],
                             start=(k == 0), stop=(k == KD - 1),
                             wait=w_, inc="E" if k == KD - 1 else False)
                    if k == KD - 1:
                        grp.append(tok)
            tok_th = op("scalar", "activation",
                        out=t["state"][:, EW * jp:EW * (jp + 1)],
                        in_=bank[:], func=AF.Tanh,
                        wait=grp + [state["state_last_read"]])
            state["tanh"][jp % 2] = tok_th
            st_toks.append(tok_th)

    def bpair(c, jp, st_toks, q1_last, A, pair=None, xbp=None):
        cp = c % 2
        xnb = t["xnb0"] if cp == 0 else t["xnb1"]
        q1 = t["q10"] if cp == 0 else t["q11"]
        ms = jp % 4
        bank2 = (t["pm2"] + t["pm1"])[ms]
        bankf = t["pff"][jp % 2]
        g2 = []
        for h in range(2):
            j = 2 * jp + h
            for k in range(KN):
                w_ = []
                if k == 0 and h == 0:
                    w_ = st_toks + [state["m2"][ms]]
                tok = op("tensor", "matmul",
                         bank2[:, T * h:T * (h + 1)],
                         t["c_bf"][:, D * k + 128 * j:D * k + 128 * (j + 1)],
                         t["state"][:, T * k:T * (k + 1)],
                         start=(k == 0), stop=(k == KN - 1),
                         wait=w_, inc="E" if k == KN - 1 else False)
                if k == KN - 1:
                    g2.append(tok)
        gf = []
        for h in range(2):
            j = 2 * jp + h
            for k in range(KD):
                w_ = []
                if k == 0 and h == 0:
                    w_ = [q1_last, state["ff"][jp % 2], WQD[0]]
                tok = op("tensor", "matmul",
                         bankf[:, T * h:T * (h + 1)],
                         t["wq"][:, D * k + 128 * j:D * k + 128 * (j + 1)],
                         q1[:, T * k:T * (k + 1)],
                         start=(k == 0), stop=(k == KD - 1),
                         wait=w_, inc="E" if k == KD - 1 else False)
                if k == KD - 1:
                    gf.append(tok)
        es = jp % 2
        tok_cp = op("scalar", "activation", out=t["me"][ms][:],
                    in_=bank2[:], func=AF.Copy,
                    wait=g2 + [me_free[ms]])
        state["m2"][ms] = tok_cp
        tok_g = op("scalar", "activation", out=t["ge"][ms][:],
                   in_=bankf[:], func=AF.Silu, scale=t["iscol"][:],
                   wait=gf + [tok_isc, ge_free[ms]])
        state["ff"][jp % 2] = tok_g
        if A is not None and pair is not None:
            xn_, v_, xs_ = etile_pair(A["c"], pair, A["bc"])
            A["xn_last"] = xn_
            A["xn_toks"].append((pair, xn_, xs_))
            q1n = t["q10"] if A["c"] % 2 == 0 else t["q11"]
            A["q_toks"].append(quant_1024(
                v_, q1n[:, 1024 * pair:1024 * (pair + 1)], QENG_R,
                extra_wait=[state["q1_free"][A["c"] % 2]]))
        if xbp is not None:
            emit_xb_pair(A, xbp)
        stoks = []
        for h in range(2):
            j = 2 * jp + h
            stoks.append(op(
                "vector", "scalar_tensor_tensor",
                out=t["me"][ms][:, T * h:T * (h + 1)],
                in0=xnb[:, EW * jp + T * h:EW * jp + T * (h + 1)],
                scalar=t["dcol"][:, j:j + 1],
                in1=t["me"][ms][:, T * h:T * (h + 1)],
                op0=ALU.mult, op1=ALU.add,
                wait=[tok_cp] + ([dtoks[-1]] if h == 0 else [])))
        tok_t = op("gpsimd", "tensor_tensor", out=t["te"][es][:],
                   in0=t["me"][ms][:], in1=t["ge"][ms][:],
                   op=ALU.mult, wait=stoks + [tok_g, te_free[es]])
        me_free[ms] = tok_t
        ge_free[ms] = tok_t
        l2 = []
        for h in range(2):
            fi = 2 * jp + h
            l2.append(dma(
                "gpsimd", f"X{es}", t["oe"][es][:, T * h:T * (h + 1)],
                hb["x_fm"][128 * fi:128 * (fi + 1), T * c:T * (c + 1)],
                wait=[oe_free[es]] if h == 0 else []))
        l2 = [l2[-1]]
        tok_o = op("gpsimd", "tensor_tensor", out=t["oe"][es][:],
                   in0=t["oe"][es][:], in1=t["te"][es][:],
                   op=ALU.add, wait=l2 + [tok_t])
        te_free[es] = tok_o
        otoks = []
        for h in range(2):
            fi = 2 * jp + h
            otoks.append(dma(
                "gpsimd", f"P{es}", hb["out"][128 * fi:128 * (fi + 1),
                                              T * c:T * (c + 1)],
                t["oe"][es][:, T * h:T * (h + 1)],
                wait=[tok_o] if h == 0 else []))
        oe_free[es] = otoks[-1]

    A0 = {"c": 0, "bc": None, "xn_last": None, "xn_toks": [],
          "xnb_toks": [], "q_toks": []}
    A1 = {"c": 1, "bc": None, "xn_last": None, "xn_toks": [],
          "xnb_toks": [], "q_toks": []}
    mv0 = mv1 = None
    vfree = [NOTOK, NOTOK]
    a0_at = {6: 0, 12: 1, 18: 2, 24: 3}
    xb_at = {9: 0, 15: 1, 21: 2, 27: 3}
    i_piece = 0
    for k in range(KD):
        for h in range(2):
            sl = (2 * k + h) % 2
            ltok = dma("sync", f"K{sl}", t["st2k"][sl][:],
                       hb["w"][128 * k:128 * (k + 1),
                               1024 * h:1024 * (h + 1)],
                       wait=[st2k_free[sl]])
            tok_v = op("scalar", "activation", out=t["xn"][sl][:],
                       in_=t["st2k"][sl][:], func=AF.Copy,
                       scale=t["s1col"][:],
                       wait=[ltok, tok_sc1, vfree[sl]])
            st2k_free[sl] = tok_v
            quant_1024(tok_v,
                       t["wq"][:, D * k + 1024 * h: D * k + 1024 * (h + 1)],
                       QENG_W, vsrc=t["xn"][sl])
            vfree[sl] = qtt[0]
            emit_bc_casts(2)
            i_piece += 1
            if i_piece == 2:
                mv0 = stats(0)
            elif i_piece == 4:
                A0["bc"] = rows_and_bcast(0, mv0)
            elif i_piece in a0_at:
                p = a0_at[i_piece]
                xn_, v_, xs_ = etile_pair(0, p, A0["bc"], xslot=2)
                A0["xn_last"] = xn_
                A0["xn_toks"].append((p, xn_, xs_))
                A0["q_toks"].append(quant_1024(
                    v_, t["q10"][:, 1024 * p:1024 * (p + 1)],
                    QENG_R, extra_wait=[state["q1_free"][0]]))
            elif i_piece in xb_at:
                emit_xb_pair(A0, xb_at[i_piece])
            elif i_piece == 30:
                mv1 = stats(1)
    WQD[0] = qfree[0]
    state["xn_read"][0] = vfree[0]
    state["xn_read"][1] = vfree[1]
    emit_bc_casts(len(pieces))
    ge_free[:] = [buf_last[0][0], buf_last[1][0],
                  buf_last[0][1], buf_last[1][1]]
    te_free[:] = list(buf_last[2])
    oe_free[:] = list(buf_last[3])
    waitonly("gpsimd", ("X0", cnt["X0"]), ("X1", cnt["X1"]))
    state["sbr_read"][0] = A0["xn_last"]
    A1["bc"] = rows_and_bcast(1, mv1)

    Acur, Anext = A0, A1
    for c in range(NCH):
        if c + 2 < NCH:
            Anext2 = {"c": c + 2, "bc": None, "xn_last": None,
                      "xn_toks": [], "xnb_toks": [], "q_toks": []}
            mvt2 = stats(c + 2)
        else:
            Anext2, mvt2 = None, None
        st_toks = []
        mm1(c, Acur["xnb_toks"][-1], (0, 1, 2, 3), st_toks)
        if Anext2 is not None:
            Anext2["bc"] = rows_and_bcast(c + 2, mvt2)
        for jp in range(KD // 2):
            pair = jp if (jp < 4 and Anext is not None) else None
            xbp = {0: 0, 4: 1, 5: 2, 6: 3}.get(jp) \
                if Anext is not None else None
            bpair(c, jp, st_toks, Acur["q_toks"][-1], Anext, pair, xbp)
        if Anext is not None:
            state["sbr_read"][(c + 1) % 2] = Anext["xn_last"]
        state["state_last_read"] = ("E", cnt["E"])
        state["xnb_free"][c % 2] = [("E", cnt["E"]), ("V", cnt["V"])]
        state["q1_free"][c % 2] = ("E", cnt["E"])
        Acur, Anext = Anext, Anext2

    waitonly("sync", *[(s, cnt[s]) for s in
                       ("SC", "CN", "K0", "K1", "F0", "F1")])
    waitonly("gpsimd", ("P0", cnt["P0"]), ("P1", cnt["P1"]),
             ("X0", cnt["X0"]), ("X1", cnt["X1"]))


_NC_CACHE = []


def kernel(L0_attn_x, L0_attn_B, L0_attn_C, L0_attn_D,
           L0_ffn_input2, L0_ffn_scale, L0_ffn_scale1):
    x = np.asarray(L0_attn_x, dtype=np.float32).reshape(NCORES * TOK, D)
    bt = np.ascontiguousarray(np.asarray(L0_attn_B, dtype=np.float32).T)
    ct = np.ascontiguousarray(np.asarray(L0_attn_C, dtype=np.float32).T)
    w = np.ascontiguousarray(np.asarray(L0_ffn_input2, dtype=np.float32))
    dv = np.asarray(L0_attn_D, dtype=np.float32).reshape(D, 1)
    s0 = np.asarray(L0_ffn_scale, dtype=np.float32).reshape(1, 1)
    s1 = np.asarray(L0_ffn_scale1, dtype=np.float32).reshape(1, 1)

    in_maps = []
    for i in range(NCORES):
        sl = x[TOK * i:TOK * (i + 1)]
        in_maps.append({
            "x_tm": np.ascontiguousarray(sl),
            "x_fm": np.ascontiguousarray(sl.T),
            "bt": bt, "ct": ct, "w": w, "dv": dv, "s0": s0, "s1": s1,
        })
    if not _NC_CACHE:
        _NC_CACHE.append(build_nc())
    nc = _NC_CACHE[0]
    res = run_bass_kernel_spmd(nc, in_maps, core_ids=list(range(NCORES)))
    outs = [np.asarray(r["out"]).T for r in res.results]
    return np.ascontiguousarray(
        np.concatenate(outs, axis=0).reshape(4, 4096, D)).astype(
            np.float32, copy=False)
